# version 09
# baseline (speedup 1.0000x reference)
import numpy as np
import ml_dtypes
from contextlib import ExitStack

import concourse.bass as bass
import concourse.mybir as mybir
from concourse.bass_utils import run_bass_kernel_spmd

F32 = mybir.dt.float32
BF16 = mybir.dt.bfloat16
I32 = mybir.dt.int32
AF = mybir.ActivationFunctionType
ALU = mybir.AluOpType
AX = mybir.AxisListType

SEQ = 2048
DM = 2048
DI = 8192
TH = 1024
NHC = 16
NIC = 64
NGRP = 4
ICG = NIC // NGRP
NQH = 32
NKV = 8
DH = 64
EPS = 1e-5
INV_DM = 1.0 / 2048.0

BF = ml_dtypes.bfloat16



def _qhead_pairs():
    return [(8 * (c // 4) + (c % 4), 8 * (c // 4) + (c % 4) + 4) for c in range(16)]


def _rope_rows(h):
    base = h * DH
    return [base + 2 * i for i in range(32)] + [base + 2 * i + 1 for i in range(32)]


def _tile4(w, n0, n1):
    return np.ascontiguousarray(w.reshape(n0, 128, n1, 128).transpose(0, 3, 2, 1))


def _rope_tables():
    d = np.arange(0, DH, 2, dtype=np.float32)
    freq = (1.0 / (10000.0 ** (d / np.float32(DH)))).astype(np.float32)
    pos = np.arange(SEQ, dtype=np.float64)
    theta = pos[None, :] * freq[:, None].astype(np.float64)
    cosF = np.tile(np.cos(theta), (4, 1)).astype(np.float32)
    sinF = np.tile(np.sin(theta), (4, 1))
    sign = np.ones((128, 1))
    sign[32:64] = -1.0
    sign[96:128] = -1.0
    return cosF, (sinF * sign).astype(np.float32)


def host_layout(inp):
    pairs = _qhead_pairs()
    qperm = np.array([r for h0, h1 in pairs for r in _rope_rows(h0) + _rope_rows(h1)])
    kperm = np.array([r for k in range(NKV) for r in _rope_rows(k)])

    wq = np.asarray(inp["L0_attn_W_q"], dtype=np.float32)[qperm]
    wkv_in = np.asarray(inp["L0_attn_W_kv"], dtype=np.float32)
    wkv = wkv_in.copy()
    wkv[0:512] = wkv_in[0:512][kperm]
    fperm = np.zeros(DM, dtype=np.int64)
    for c in range(16):
        h0, h1 = pairs[c]
        fperm[c * 128: c * 128 + 64] = np.arange(64) + h0 * 64
        fperm[c * 128 + 64: (c + 1) * 128] = np.arange(64) + h1 * 64
    wout = np.asarray(inp["L0_attn_W_out"], dtype=np.float32)[:, fperm]

    bq = np.asarray(inp["L0_attn_b_q"], dtype=np.float32)[qperm].reshape(16, 128).T
    bkv_in = np.asarray(inp["L0_attn_b_kv"], dtype=np.float32)
    bkv = bkv_in.copy()
    bkv[0:512] = bkv_in[0:512][kperm]
    bkv = bkv.reshape(8, 128).T
    bout = np.asarray(inp["L0_attn_b_out"], dtype=np.float32).reshape(16, 128).T
    sg = np.asarray(inp["L0_ffn_gate_scale"], dtype=np.float32).reshape(64, 128).T
    su = np.asarray(inp["L0_ffn_up_scale"], dtype=np.float32).reshape(64, 128).T
    sd = np.asarray(inp["L0_ffn_down_scale"], dtype=np.float32).reshape(16, 128).T
    smalls = np.ascontiguousarray(
        np.concatenate([bq, bkv, bout, sg, su, sd], axis=1).astype(np.float32))

    gw = _tile4(np.asarray(inp["L0_ffn_gate_w"], dtype=np.int32), NIC, NHC)
    uw = _tile4(np.asarray(inp["L0_ffn_up_w"], dtype=np.int32), NIC, NHC)
    dw = _tile4(np.asarray(inp["L0_ffn_down_w"], dtype=np.int32), NHC, NIC)
    dw = np.ascontiguousarray(
        dw.reshape(NHC, 128, NGRP, ICG, 128).transpose(2, 0, 1, 3, 4))

    cosF, sinT = _rope_tables()
    esel = np.zeros((32, 16 * 128), dtype=np.float32)
    for c in range(16):
        esel[2 * c, c * 128: c * 128 + 64] = 1.0
        esel[2 * c + 1, c * 128 + 64: (c + 1) * 128] = 1.0

    return dict(
        smalls=smalls,
        gwt=np.ascontiguousarray(gw.reshape(DI, DM)),
        uwt=np.ascontiguousarray(uw.reshape(DI, DM)),
        dwt=np.ascontiguousarray(dw.reshape(NGRP * NHC * 128, ICG * 128)),
        wqt=np.ascontiguousarray(_tile4(wq, 16, 16).reshape(DM, DM)),
        wkvt=np.ascontiguousarray(_tile4(wkv, 8, 16).reshape(1024, DM)),
        woutt=np.ascontiguousarray(_tile4(wout, 16, 16).reshape(DM, DM)),
        cosF=cosF, sinT=sinT, esel=esel,
        ident=np.eye(128, dtype=np.float32),
    )




class Sem:
    def __init__(self, h):
        self.h = h
        self.n = 0

    def inc(self, instr, k=1):
        instr.then_inc(self.h, k)
        self.n += k
        return self.n

    def expect(self, base, add, tag):
        assert self.n == base + add, f"{tag}: sem {self.h.name} n={self.n} != {base}+{add}"


def build(dbg=None):
    nc = bass.Bass("TRN2", target_bir_lowering=False)

    xT = nc.dram_tensor("xT", [DM, SEQ], F32, kind="ExternalInput")
    gwt_d = nc.dram_tensor("gwt", [DI, DM], I32, kind="ExternalInput")
    uwt_d = nc.dram_tensor("uwt", [DI, DM], I32, kind="ExternalInput")
    dwt_d = nc.dram_tensor("dwt", [NGRP * NHC * 128, ICG * 128], I32, kind="ExternalInput")
    wqt_d = nc.dram_tensor("wqt", [DM, DM], F32, kind="ExternalInput")
    wkvt_d = nc.dram_tensor("wkvt", [1024, DM], F32, kind="ExternalInput")
    woutt_d = nc.dram_tensor("woutt", [DM, DM], F32, kind="ExternalInput")
    smalls_d = nc.dram_tensor("smalls", [128, 184], F32, kind="ExternalInput")
    cosF_d = nc.dram_tensor("cosF", [128, SEQ], F32, kind="ExternalInput")
    sinT_d = nc.dram_tensor("sinT", [128, SEQ], F32, kind="ExternalInput")
    esel_d = nc.dram_tensor("esel", [32, 16 * 128], F32, kind="ExternalInput")
    ident_d = nc.dram_tensor("ident", [128, 128], F32, kind="ExternalInput")
    out_d = nc.dram_tensor("outT", [DM, SEQ], F32, kind="ExternalOutput")

    rk1 = "ExternalOutput" if dbg in ("f1", "attn") else "Internal"
    rk2 = "ExternalOutput" if dbg == "attn" else "Internal"
    r1_d = nc.dram_tensor("r1", [DM, SEQ], F32, kind=rk1)
    r2_d = nc.dram_tensor("r2", [DM, SEQ], F32, kind=rk2)
    sall_d = nc.dram_tensor("sall", [32, SEQ], F32, kind="Internal")
    qr_d = nc.dram_tensor("qr", [DM, SEQ], BF16, kind="Internal")

    ctx = ExitStack()
    with ctx:
        QW = Sem(ctx.enter_context(nc.semaphore("QW")))
        QL = Sem(ctx.enter_context(nc.semaphore("QL")))
        QS = Sem(ctx.enter_context(nc.semaphore("QS")))
        PE = Sem(ctx.enter_context(nc.semaphore("PE")))
        AC = Sem(ctx.enter_context(nc.semaphore("AC")))
        DV = Sem(ctx.enter_context(nc.semaphore("DV")))

        sbg = lambda n, s, d: ctx.enter_context(nc.sbuf_tensor(n, s, d))
        ones128 = sbg("ones128", [128, 1], BF16)
        onesg = sbg("onesg", [128, 128], BF16)
        onesrow = onesg[0:1, :]
        onesrow64 = onesg[64:65, :]
        neg128row = sbg("neg128row", [1, 128], BF16)
        identb = sbg("identb", [128, 128], BF16)
        epst = sbg("epst", [128, 1], F32)
        smalls = sbg("smallst", [128, 184], F32)
        halfsd = sbg("halfsd", [128, 16], F32)

        def sm(col):
            return smalls[:, col: col + 1]

        with nc.Block() as block:
            @block.sync
            def _(sync):
                QL.inc(sync.dma_start(out=smalls[:, :], in_=smalls_d[:, :]), 16)

            @block.gpsimd
            def _(g):
                QW.inc(g.dma_start(out=identb[:, :], in_=ident_d[:, :]), 16)

            @block.vector
            def _(v):
                v.wait_ge(QL.h, 16)
                v.memset(ones128[:, :], 1.0)
                v.memset(onesg[:, :], 1.0)
                v.memset(neg128row[:, :], -128.0)
                v.memset(epst[:, :], EPS)
                v.tensor_scalar_mul(halfsd[:, :], smalls[:, 168:184], 0.5)

        def ln_full(tag, src, xview, lnb, wait_qs=None):
            (xc, xq, xqs, mb, rb, tbuf, mrow, trow, rstd, rowsb) = lnb

            def RB(j):
                return rowsb[j][0:1, :]

            with nc.Block() as block:
                ql0, pe0, ac0, dv0 = QL.n, PE.n, AC.n, DV.n

                @block.sync
                def _(sync):
                    if wait_qs is not None:
                        sync.wait_ge(QS.h, wait_qs)
                    for dc in range(NHC):
                        if dc >= 2:
                            sync.wait_ge(DV.h, dv0 + 2 * dc - 3)
                        QL.inc(sync.dma_start(
                            out=xc[dc % 2][:, :],
                            in_=src[dc * 128:(dc + 1) * 128, :]), 16)

                with nc.psum_tensor(f"{tag}psx", [1, SEQ], F32) as psx, \
                     nc.psum_tensor(f"{tag}psq", [1, SEQ], F32) as psq:

                    @block.tensor
                    def _(t):
                        for dc in range(NHC):
                            t.wait_ge(DV.h, dv0 + 2 * dc + 1)
                            for j in range(4):
                                t.matmul(psx[0:1, j * 512:(j + 1) * 512],
                                         ones128[:, :],
                                         xq[dc % 2][:, j * 512:(j + 1) * 512],
                                         start=(dc == 0), stop=(dc == NHC - 1))
                            t.wait_ge(DV.h, dv0 + 2 * dc + 2)
                            last = None
                            for j in range(4):
                                last = t.matmul(psq[0:1, j * 512:(j + 1) * 512],
                                                ones128[:, :],
                                                xqs[dc % 2][:, j * 512:(j + 1) * 512],
                                                start=(dc == 0), stop=(dc == NHC - 1))
                            PE.inc(last)

                    @block.scalar
                    def _(s):
                        s.wait_ge(DV.h, dv0 + 2 * NHC + 1)
                        s.activation(trow[0:1, :], trow[0:1, :], AF.Ln,
                                     bias=epst[0:1, 0:1])
                        AC.inc(s.activation(rstd[0:1, :], trow[0:1, :], AF.Exp,
                                            scale=-0.5))

                    @block.vector
                    def _(v):
                        for dc in range(NHC):
                            v.wait_ge(QL.h, ql0 + 16 * (dc + 1))
                            if dc >= 2:
                                v.wait_ge(PE.h, pe0 + dc - 1)
                            DV.inc(v.tensor_copy(xq[dc % 2][:, :], xc[dc % 2][:, :]))
                            DV.inc(v.tensor_mul(xqs[dc % 2][:, :], xq[dc % 2][:, :],
                                                xq[dc % 2][:, :]))
                        v.wait_ge(PE.h, pe0 + NHC)
                        v.wait_ge(AC.h, ac0)
                        v.tensor_scalar_mul(mrow[0:1, :], psx[0:1, :], INV_DM)
                        v.tensor_scalar_mul(trow[0:1, :], psq[0:1, :], INV_DM)
                        v.tensor_mul(tbuf[64:65, 0:SEQ], mrow[0:1, :], mrow[0:1, :])
                        DV.inc(v.tensor_sub(trow[0:1, :], trow[0:1, :],
                                            tbuf[64:65, 0:SEQ]))
                        v.wait_ge(AC.h, ac0 + 1)
                        v.tensor_copy(RB(0), mrow[0:1, :])
                        DV.inc(v.tensor_copy(RB(1), rstd[0:1, :]))

                with nc.psum_tensor(f"{tag}pmb", [128, SEQ], F32) as pmb, \
                     nc.psum_tensor(f"{tag}prb", [128, SEQ], F32) as prb:

                    @block.tensor
                    def _(t):
                        t.wait_ge(DV.h, dv0 + 2 * NHC + 2)
                        for j in range(4):
                            sl = slice(j * 512, (j + 1) * 512)
                            t.matmul(pmb[:, sl], onesrow[:, :], RB(0)[:, sl],
                                     start=True, stop=True)
                            last = t.matmul(prb[:, sl], onesrow[:, :], RB(1)[:, sl],
                                            start=True, stop=True)
                        PE.inc(last)

                    @block.scalar
                    def _(s):
                        s.wait_ge(PE.h, pe0 + NHC + 1)
                        s.activation(mb[:, :], pmb[:, :], AF.Copy)
                        AC.inc(s.activation(rb[:, :], prb[:, :], AF.Copy))

                QL.expect(ql0, 256, tag)
                PE.expect(pe0, NHC + 1, tag)
                AC.expect(ac0, 2, tag)
                DV.expect(dv0, 2 * NHC + 2, tag)

            with nc.Block() as block:
                ql0, dv0 = QL.n, DV.n

                @block.sync
                def _(sync):
                    for dc in range(NHC):
                        if dc >= 2:
                            sync.wait_ge(DV.h, dv0 + 2 * dc - 2)
                        QL.inc(sync.dma_start(
                            out=xc[dc % 2][:, :],
                            in_=src[dc * 128:(dc + 1) * 128, :]), 16)

                @block.vector
                def _(v):
                    for dc in range(NHC):
                        v.wait_ge(QL.h, ql0 + 16 * (dc + 1))
                        DV.inc(v.tensor_sub(tbuf[:, :], xc[dc % 2][:, :], mb[:, :]))
                        DV.inc(v.tensor_mul(xview(dc), tbuf[:, :], rb[:, :]))

                QL.expect(ql0, 256, tag + "n")
                DV.expect(dv0, 32, tag + "n")

        def mk_lnb(pfx, c):
            lsb = lambda n, s, d: c.enter_context(nc.sbuf_tensor(pfx + n, s, d))
            prow1 = lsb("prow1", [128, SEQ], F32)
            rowb1 = lsb("rowb1", [1, SEQ], BF16)
            rowb2 = lsb("rowb2", [1, SEQ], BF16)
            tbuf = lsb("tbuf", [128, SEQ], F32)
            return (
                [lsb(f"xc{i}", [128, SEQ], F32) for i in range(2)],
                [lsb(f"xq{i}", [128, SEQ], BF16) for i in range(2)],
                [lsb(f"xqs{i}", [128, SEQ], BF16) for i in range(2)],
                lsb("mb", [128, SEQ], F32),
                lsb("rb", [128, SEQ], F32),
                tbuf,
                prow1[0:1, :],
                prow1[64:65, :],
                tbuf[64:65, :],
                (rowb1, rowb2),
            )

        def ffn_phase(src, dst, tag, wait_qs):
            fctx = ExitStack()
            with fctx:
                fsb = lambda n, s, d: fctx.enter_context(nc.sbuf_tensor(f"{tag}{n}", s, d))
                xhat = fsb("xhat", [128, NHC * SEQ], BF16)
                hbuf = fsb("h", [128, ICG * SEQ], BF16)

                def xview(dc):
                    return xhat[:, dc * SEQ:(dc + 1) * SEQ]

                def hview(icg):
                    return hbuf[:, icg * SEQ:(icg + 1) * SEQ]

                with ExitStack() as lctx:
                    lnb = mk_lnb(f"{tag}l", lctx)
                    ln_full(f"{tag}l", src, xview, lnb, wait_qs)

                with ExitStack() as wctx:
                    wsb_ = lambda n, s, d: wctx.enter_context(
                        nc.sbuf_tensor(f"{tag}{n}", s, d))
                    gsb = [wsb_(f"gwb{i}", [128, SEQ], BF16) for i in range(2)]
                    usb = [wsb_(f"uwb{i}", [128, SEQ], BF16) for i in range(2)]
                    dsb = [wsb_(f"dwb{i}", [128, SEQ], BF16) for i in range(2)]
                    gs_buf = wsb_("gs", [128, SEQ], BF16)
                    us_buf = wsb_("us", [128, SEQ], BF16)
                    rsrc = [wsb_(f"rs{i}", [128, SEQ], F32) for i in range(2)]
                    rout = [wsb_(f"ro{i}", [128, SEQ], F32) for i in range(2)]
                    pf = {"qw": None}

                    for grp in range(NGRP):
                        with nc.Block() as block, \
                             nc.psum_tensor(f"{tag}{grp}pg", [128, SEQ], F32) as pg, \
                             nc.psum_tensor(f"{tag}{grp}pu", [128, SEQ], F32) as pu:
                            qw0, pe0, ac0, dv0 = QW.n, PE.n, AC.n, DV.n

                            def dv_c(i):
                                return dv0 + (1 if i == 0 else 2 * i)

                            def dv_h(i):
                                return dv0 + (2 * i + 3 if i <= ICG - 2 else 2 * ICG)

                            pfq = pf["qw"]
                            pf["qw"] = None

                            def qw_land(icg):
                                if pfq is not None:
                                    return pfq[icg] if icg < 2 else qw0 + 32 * (icg - 1)
                                return qw0 + 32 * (icg + 1)

                            @block.gpsimd
                            def _(g):
                                for icg in range(ICG):
                                    gic = grp * ICG + icg
                                    if pfq is not None and icg < 2:
                                        continue
                                    if icg >= 2:
                                        g.wait_ge(PE.h, pe0 + 2 * icg - 2)
                                    QW.inc(g.dma_start(
                                        out=gsb[icg % 2][:, :],
                                        in_=gwt_d[gic * 128:(gic + 1) * 128, :]), 16)
                                    QW.inc(g.dma_start(
                                        out=usb[icg % 2][:, :],
                                        in_=uwt_d[gic * 128:(gic + 1) * 128, :]), 16)

                            @block.tensor
                            def _(t):
                                for icg in range(ICG):
                                    t.wait_ge(DV.h, dv_c(icg))
                                    if icg >= 1:
                                        t.wait_ge(AC.h, ac0 + 2 * icg - 1)
                                    last = None
                                    for hc in range(NHC):
                                        for j in range(4):
                                            last = t.matmul(
                                                pg[:, j * 512:(j + 1) * 512],
                                                gsb[icg % 2][:, hc * 128:(hc + 1) * 128],
                                                xview(hc)[:, j * 512:(j + 1) * 512],
                                                start=(hc == 0), stop=(hc == NHC - 1))
                                    PE.inc(last)
                                    if icg >= 1:
                                        t.wait_ge(AC.h, ac0 + 2 * icg)
                                    last = None
                                    for hc in range(NHC):
                                        for j in range(4):
                                            last = t.matmul(
                                                pu[:, j * 512:(j + 1) * 512],
                                                usb[icg % 2][:, hc * 128:(hc + 1) * 128],
                                                xview(hc)[:, j * 512:(j + 1) * 512],
                                                start=(hc == 0), stop=(hc == NHC - 1))
                                    PE.inc(last)

                            @block.scalar
                            def _(s):
                                for icg in range(ICG):
                                    gic = grp * ICG + icg
                                    s.wait_ge(PE.h, pe0 + 2 * icg + 1)
                                    if icg >= 1:
                                        s.wait_ge(DV.h, dv_h(icg - 1))
                                    AC.inc(s.activation(gs_buf[:, :], pg[:, :],
                                                        AF.Silu, scale=sm(40 + gic)))
                                    s.wait_ge(PE.h, pe0 + 2 * icg + 2)
                                    AC.inc(s.activation(us_buf[:, :], pu[:, :],
                                                        AF.Identity,
                                                        scale=sm(104 + gic)))

                            @block.vector
                            def _(v):
                                for icg in range(ICG):
                                    v.wait_ge(QW.h, qw_land(icg))
                                    v.tensor_scalar_add(gsb[icg % 2][:, :],
                                                        gsb[icg % 2][:, :], -128.0)
                                    DV.inc(v.tensor_scalar_add(
                                        usb[icg % 2][:, :], usb[icg % 2][:, :],
                                        -128.0))
                                    if icg >= 1:
                                        v.wait_ge(AC.h, ac0 + 2 * icg)
                                        DV.inc(v.tensor_mul(hview(icg - 1),
                                                            gs_buf[:, :],
                                                            us_buf[:, :]))
                                v.wait_ge(AC.h, ac0 + 2 * ICG)
                                DV.inc(v.tensor_mul(hview(ICG - 1), gs_buf[:, :],
                                                    us_buf[:, :]))

                            QW.expect(qw0, 32 * ICG - (64 if pfq is not None else 0),
                                      tag + "gu")
                            PE.expect(pe0, 2 * ICG, tag + "gu")
                            AC.expect(ac0, 2 * ICG, tag + "gu")
                            DV.expect(dv0, 2 * ICG, tag + "gu")

                        with nc.Block() as block, \
                             nc.psum_tensor(f"{tag}{grp}pd0", [128, SEQ], F32) as pd0, \
                             nc.psum_tensor(f"{tag}{grp}pd1", [128, SEQ], F32) as pd1:
                            pd = [pd0, pd1]
                            qw0, ql0, qs0 = QW.n, QL.n, QS.n
                            pe0, ac0, dv0 = PE.n, AC.n, DV.n
                            rsrc_t = src if grp == 0 else dst

                            def dv_c(i):
                                return dv0 + (1 if i == 0 else 2 * i)

                            def dv_stt(i):
                                return dv0 + (2 * i + 3 if i <= NHC - 2 else 2 * NHC)

                            @block.gpsimd
                            def _(g):
                                for hc in range(NHC):
                                    if hc >= 2:
                                        g.wait_ge(PE.h, pe0 + hc - 1)
                                    r0 = (grp * NHC + hc) * 128
                                    QW.inc(g.dma_start(
                                        out=dsb[hc % 2][:, :],
                                        in_=dwt_d[r0:r0 + 128, :]), 16)
                                if grp < NGRP - 1:
                                    qwl = []
                                    for icg in range(2):
                                        gic = (grp + 1) * ICG + icg
                                        QW.inc(g.dma_start(
                                            out=gsb[icg][:, :],
                                            in_=gwt_d[gic * 128:(gic + 1) * 128, :]), 16)
                                        qwl.append(QW.inc(g.dma_start(
                                            out=usb[icg][:, :],
                                            in_=uwt_d[gic * 128:(gic + 1) * 128, :]), 16))
                                    pf["qw"] = qwl

                            @block.sync
                            def _(sync):
                                for hc in range(NHC):
                                    if hc == 0 and grp > 0:
                                        sync.wait_ge(QS.h, qs0)
                                    if hc >= 2:
                                        sync.wait_ge(DV.h, dv0 + 2 * hc - 1)
                                    QL.inc(sync.dma_start(
                                        out=rsrc[hc % 2][:, :],
                                        in_=rsrc_t[hc * 128:(hc + 1) * 128, :]), 16)

                            @block.vector
                            def _(v):
                                for hc in range(NHC):
                                    v.wait_ge(QW.h, qw0 + 16 * (hc + 1))
                                    DV.inc(v.tensor_scalar_add(
                                        dsb[hc % 2][:, :], dsb[hc % 2][:, :],
                                        -128.0))
                                    if hc >= 1:
                                        v.wait_ge(PE.h, pe0 + hc)
                                        v.wait_ge(QL.h, ql0 + 16 * hc)
                                        if hc >= 3:
                                            v.wait_ge(QS.h, qs0 + 16 * (hc - 2))
                                        DV.inc(v.scalar_tensor_tensor(
                                            rout[(hc - 1) % 2][:, :],
                                            pd[(hc - 1) % 2][:, :],
                                            halfsd[:, hc - 1:hc],
                                            rsrc[(hc - 1) % 2][:, :],
                                            ALU.mult, ALU.add))
                                v.wait_ge(PE.h, pe0 + NHC)
                                v.wait_ge(QL.h, ql0 + 16 * NHC)
                                v.wait_ge(QS.h, qs0 + 16 * (NHC - 2))
                                DV.inc(v.scalar_tensor_tensor(
                                    rout[(NHC - 1) % 2][:, :],
                                    pd[(NHC - 1) % 2][:, :],
                                    halfsd[:, NHC - 1:NHC],
                                    rsrc[(NHC - 1) % 2][:, :],
                                    ALU.mult, ALU.add))

                            @block.tensor
                            def _(t):
                                for hc in range(NHC):
                                    t.wait_ge(DV.h, dv_c(hc))
                                    if hc >= 2:
                                        t.wait_ge(DV.h, dv0 + 2 * hc - 1)
                                    last = None
                                    for icg in range(ICG):
                                        for j in range(4):
                                            last = t.matmul(
                                                pd[hc % 2][:, j * 512:(j + 1) * 512],
                                                dsb[hc % 2][:, icg * 128:(icg + 1) * 128],
                                                hview(icg)[:, j * 512:(j + 1) * 512],
                                                start=(icg == 0), stop=(icg == ICG - 1))
                                    PE.inc(last)

                            @block.scalar
                            def _(s):
                                for hc in range(NHC):
                                    s.wait_ge(DV.h, dv_stt(hc))
                                    QS.inc(s.dma_start(
                                        out=dst[hc * 128:(hc + 1) * 128, :],
                                        in_=rout[hc % 2][:, :]), 16)

                            QW.expect(qw0, 16 * NHC + (64 if grp < NGRP - 1 else 0),
                                      tag + "dn")
                            QL.expect(ql0, 16 * NHC, tag + "dn")
                            QS.expect(qs0, 16 * NHC, tag + "dn")
                            PE.expect(pe0, NHC, tag + "dn")
                            DV.expect(dv0, 2 * NHC, tag + "dn")
            return

        ffn_phase(xT, r1_d, "f1", wait_qs=None)
        qs_after_f1 = QS.n

        if dbg == "f1":
            with nc.Block() as block:
                @block.sync
                def _(sync):
                    sync.wait_ge(QS.h, QS.n)
                    sync.wait_ge(QW.h, QW.n)
                    sync.wait_ge(QL.h, QL.n)
            return nc

        actx = ExitStack()
        with actx:
            asb = lambda n, s, d: actx.enter_context(nc.sbuf_tensor(n, s, d))
            Kr = asb("Kr", [128, 4 * SEQ], BF16)
            Vd = asb("Vd", [128, 8 * 16 * 65], BF16)

            def krview(kc2):
                return Kr[:, kc2 * SEQ:(kc2 + 1) * SEQ]

            def vdview(kv, kc):
                return Vd[:, kv * 1040 + kc * 65: kv * 1040 + (kc + 1) * 65]

            with ExitStack() as sctx:
                ssb = lambda n, s, d: sctx.enter_context(nc.sbuf_tensor(n, s, d))
                xhat2 = ssb("xhat2", [128, NHC * SEQ], BF16)

                def x2view(dc):
                    return xhat2[:, dc * SEQ:(dc + 1) * SEQ]

                with ExitStack() as lctx:
                    lnb = mk_lnb("a_l", lctx)
                    ln_full("a_l", r1_d, x2view, lnb, wait_qs=qs_after_f1)

                cosT = ssb("cosT", [128, SEQ], F32)
                sinTt = ssb("sinTt", [128, SEQ], F32)
                wsb = [ssb(f"wqkv{i}", [128, SEQ], BF16) for i in range(2)]
                qbf = ssb("qbf", [128, SEQ], F32)
                rot2 = ssb("rot2", [128, SEQ], F32)
                rotq = [ssb(f"rotq{i}", [128, SEQ], BF16) for i in range(2)]
                vtmp = [ssb("vtmp0", [128, SEQ], BF16)]
                vtmp.append(vtmp[0])

                with nc.Block() as block:
                    @block.sync
                    def _(sync):
                        QL.inc(sync.dma_start(out=cosT[:, :], in_=cosF_d[:, :]), 16)
                        QL.inc(sync.dma_start(out=sinTt[:, :], in_=sinT_d[:, :]), 16)

                with nc.Block() as block, \
                     nc.psum_tensor("pqkv", [128, SEQ], F32) as pq, \
                     nc.psum_tensor("ptv0", [128, 64], BF16) as ptv0, \
                     nc.psum_tensor("ptv1", [128, 64], BF16) as ptv1:
                    ptv = [ptv0, ptv1]
                    qw0, qs0, pe0, ac0, dv0 = QW.n, QS.n, PE.n, AC.n, DV.n
                    NOC = 24

                    def isV(oc):
                        return oc >= 20

                    def pe_mm(oc):
                        return pe0 + sum(1 + (32 if isV(o) else 0) for o in range(oc)) + 1

                    def pe_tr(oc, i):
                        return pe_mm(oc) + i + 1

                    def ac_evac(oc):
                        return ac0 + sum(1 + (32 if isV(o) else 0) for o in range(oc)) + 1

                    def ac_vd(oc, i):
                        return ac_evac(oc) + i + 1

                    def dv_rope(oc):
                        return dv0 + sum(1 for o in range(oc + 1) if not isV(o))

                    @block.gpsimd
                    def _(g):
                        for oc in range(NOC):
                            if oc >= 2:
                                g.wait_ge(PE.h, pe_mm(oc - 2))
                            src_w = wqt_d if oc < 16 else wkvt_d
                            r0 = oc * 128 if oc < 16 else (oc - 16) * 128
                            QW.inc(g.dma_start(out=wsb[oc % 2][:, :],
                                               in_=src_w[r0:r0 + 128, :]), 16)

                    @block.tensor
                    def _(t):
                        for oc in range(NOC):
                            t.wait_ge(QW.h, qw0 + 16 * (oc + 1))
                            if oc >= 1:
                                t.wait_ge(AC.h, ac_evac(oc - 1))
                            last = None
                            for dc in range(NHC):
                                for j in range(4):
                                    last = t.matmul(
                                        pq[:, j * 512:(j + 1) * 512],
                                        wsb[oc % 2][:, dc * 128:(dc + 1) * 128],
                                        xhat2[:, dc * SEQ + j * 512: dc * SEQ + (j + 1) * 512],
                                        start=(dc == 0), stop=(dc == NHC - 1))
                            PE.inc(last)
                            if isV(oc):
                                kv2 = oc - 20
                                t.wait_ge(AC.h, ac_evac(oc))
                                for i in range(32):
                                    kvh = kv2 * 2 + i // 16
                                    kc = i % 16
                                    b = (i // 16) * 64
                                    if i >= 2:
                                        t.wait_ge(AC.h, ac_vd(oc, i - 2))
                                    PE.inc(t.transpose(
                                        ptv[i % 2][:, :],
                                        vtmp[oc % 2][b:b + 64, kc * 128:(kc + 1) * 128],
                                        identb[b:b + 64, b:b + 64]))

                    @block.scalar
                    def _(s):
                        for oc in range(NOC):
                            s.wait_ge(PE.h, pe_mm(oc))
                            if oc < 20:
                                if oc >= 1 and not isV(oc - 1):
                                    nqk = sum(1 for o in range(oc) if not isV(o))
                                    s.wait_ge(DV.h, dv0 + nqk)
                                AC.inc(s.activation(qbf[:, :], pq[:, :], AF.Identity,
                                                    bias=sm(oc)))
                            else:
                                if oc >= 21:
                                    s.wait_ge(PE.h, pe_tr(oc - 1, 31))
                                AC.inc(s.activation(vtmp[oc % 2][:, :], pq[:, :],
                                                    AF.Identity, bias=sm(oc)))
                                for i in range(32):
                                    kvh = (oc - 20) * 2 + i // 16
                                    kc = i % 16
                                    s.wait_ge(PE.h, pe_tr(oc, i))
                                    AC.inc(s.activation(
                                        vdview(kvh, kc)[:, 0:64], ptv[i % 2][:, :],
                                        AF.Copy))
                            if 1 <= oc <= 16 and (oc - 1) < 16:
                                s.wait_ge(DV.h, dv_rope(oc - 1))
                                QS.inc(s.dma_start(
                                    out=qr_d[(oc - 1) * 128: oc * 128, :],
                                    in_=rotq[(oc - 1) % 2][:, :]), 16)

                    @block.vector
                    def _(v):
                        for kv in range(8):
                            for kc in range(16):
                                v.memset(vdview(kv, kc)[:, 64:65], 1.0)
                        for oc in range(20):
                            v.wait_ge(AC.h, ac_evac(oc))
                            if oc < 16 and oc >= 2:
                                v.wait_ge(QS.h, qs0 + 16 * (oc - 1))
                            for (d0, s0) in ((0, 32), (32, 0), (64, 96), (96, 64)):
                                v.tensor_mul(rot2[d0:d0 + 32, :],
                                             qbf[s0:s0 + 32, :], sinTt[s0:s0 + 32, :])
                            v.tensor_mul(qbf[:, :], qbf[:, :], cosT[:, :])
                            dest = rotq[oc % 2][:, :] if oc < 16 \
                                else krview(oc - 16)
                            DV.inc(v.tensor_add(dest, qbf[:, :], rot2[:, :]))

                    QW.expect(qw0, 16 * NOC, "qkv")
                    PE.expect(pe0, NOC + 4 * 32, "qkv")
                    AC.expect(ac0, NOC + 4 * 32, "qkv")
                    DV.expect(dv0, 20, "qkv")
                    QS.expect(qs0, 16 * 16, "qkv")


            qs_after_qkv = QS.n

            with ExitStack() as sctx:
                ssb = lambda n, s, d: sctx.enter_context(nc.sbuf_tensor(n, s, d))
                attn = ssb("attn", [128, NHC * SEQ], BF16)
                PbA = [ssb(f"PbA{i}", [128, TH], BF16) for i in range(2)]
                PbB = [ssb(f"PbB{i}", [128, TH], BF16) for i in range(2)]
                qchunk = [ssb(f"qch{i}", [128, SEQ], BF16) for i in range(2)]
                kbars = ssb("kbars", [128, 4], BF16)
                kbf32 = ssb("kbf32", [128, 4], F32)
                negb_all = ssb("negball", [128, 32], F32)
                statr = ssb("statr", [1, 64], F32)
                statb = ssb("statb", [1, 32], BF16)
                srowA = ssb("srowA", [1, SEQ], F32)
                srowB = ssb("srowB", [1, SEQ], F32)
                SallT = ssb("SallT", [32, SEQ], F32)
                SinvT = ssb("SinvT", [32, SEQ], F32)
                Shi = ssb("Shi", [32, SEQ], BF16)
                Slo = ssb("Slo", [32, SEQ], BF16)
                eselb = ssb("eselb", [32, 16 * 128], BF16)

                def attnview(c):
                    return attn[:, c * SEQ:(c + 1) * SEQ]

                with nc.Block() as block, \
                     nc.psum_tensor("pshA", [128, TH], F32) as pshA, \
                     nc.psum_tensor("pshB", [128, TH], F32) as pshB, \
                     nc.psum_tensor("pavA", [65, TH], F32) as pavA, \
                     nc.psum_tensor("pavB", [65, TH], F32) as pavB:
                    qw0, ql0, qs0, pe0, ac0, dv0 = QW.n, QL.n, QS.n, PE.n, AC.n, DV.n

                    @block.sync
                    def _(sync):
                        for c in range(16):
                            if c == 0:
                                sync.wait_ge(QS.h, qs0)
                            if c >= 2:
                                sync.wait_ge(PE.h, pe0 + 2 * c - 2)
                            QL.inc(sync.dma_start(
                                out=qchunk[c % 2][:, :],
                                in_=qr_d[c * 128:(c + 1) * 128, :]), 16)
                        for c in range(16):
                            sync.wait_ge(PE.h, pe0 + 34 if c < 2
                                         else pe0 + 34 + 128 * (c - 1))
                            QL.inc(sync.dma_start(
                                out=qchunk[c % 2][:, :],
                                in_=qr_d[c * 128:(c + 1) * 128, :]), 16)

                    @block.vector
                    def _(v):
                        for kc2 in range(4):
                            v.reduce_sum(kbf32[:, kc2:kc2 + 1],
                                         krview(kc2), axis=AX.X)
                        last = None
                        for kc2 in range(4):
                            last = v.tensor_scalar_mul(kbars[:, kc2:kc2 + 1],
                                                       kbf32[:, kc2:kc2 + 1], INV_DM)
                        DV.inc(last)
                        for j in range(NQH):
                            r = (j % 2) * 32
                            v.wait_ge(PE.h, pe0 + j + 1)
                            v.reduce_max(statr[0:1, 0:1], pshA[r:r + 1, 0:TH], axis=AX.X)
                            v.reduce_max(statr[0:1, 1:2], pshB[r:r + 1, 0:TH], axis=AX.X)
                            v.tensor_reduce(statr[0:1, 2:3], statr[0:1, 0:2],
                                            AX.X, ALU.max, negate=True)
                            DV.inc(v.tensor_scalar_mul(statb[0:1, j:j + 1],
                                                       statr[0:1, 2:3], 0.125))
                        for c in range(16):
                            pc = pe0 + 34 + 128 * c
                            for qh in range(2):
                                t0_ = qh * TH
                                v.wait_ge(PE.h, pc + 64 * qh + 63)
                                DV.inc(v.tensor_copy(
                                    attnview(c)[0:64, t0_:t0_ + TH], pavA[0:64, :]))
                                v.wait_ge(PE.h, pc + 64 * qh + 64)
                                DV.inc(v.tensor_copy(
                                    attnview(c)[64:128, t0_:t0_ + TH], pavB[0:64, :]))
                                if qh == 0 and c >= 1:
                                    v.wait_ge(QW.h, qw0 + 32 * c - 16)
                                DV.inc(v.tensor_copy(srowA[0:1, t0_:t0_ + TH],
                                                     pavA[64:65, :]))
                                if qh == 0 and c >= 1:
                                    v.wait_ge(QW.h, qw0 + 32 * c)
                                DV.inc(v.tensor_copy(srowB[0:1, t0_:t0_ + TH],
                                                     pavB[64:65, :]))

                    @block.tensor
                    def _(t):
                        t.wait_ge(DV.h, dv0 + 1)
                        for j in range(NQH):
                            c, slot = j // 2, j % 2
                            qb = slot * 64
                            kv = 2 * (c // 4) + slot
                            kc2 = kv // 2
                            r = (j % 2) * 32
                            t.wait_ge(QL.h, ql0 + 16 * (c + 1))
                            if j >= 2:
                                t.wait_ge(DV.h, dv0 + j)
                            t.matmul(pshA[r:r + 1, 0:512], kbars[qb:qb + 64, kc2:kc2 + 1],
                                     qchunk[c % 2][qb:qb + 64, 0:512],
                                     start=True, stop=True)
                            t.matmul(pshA[r:r + 1, 512:1024], kbars[qb:qb + 64, kc2:kc2 + 1],
                                     qchunk[c % 2][qb:qb + 64, 512:1024],
                                     start=True, stop=True)
                            t.matmul(pshB[r:r + 1, 0:512], kbars[qb:qb + 64, kc2:kc2 + 1],
                                     qchunk[c % 2][qb:qb + 64, 1024:1536],
                                     start=True, stop=True)
                            last = t.matmul(pshB[r:r + 1, 512:1024],
                                            kbars[qb:qb + 64, kc2:kc2 + 1],
                                            qchunk[c % 2][qb:qb + 64, 1536:2048],
                                            start=True, stop=True)
                            PE.inc(last)
                        t.wait_ge(DV.h, dv0 + 33)
                        for j in range(NQH):
                            ins = t.matmul(pshA[0:128, j:j + 1], onesrow[:, :],
                                           statb[0:1, j:j + 1],
                                           start=True, stop=True)
                            if j >= NQH - 2:
                                PE.inc(ins)
                        t.wait_ge(AC.h, ac0 + 1)
                        ql1 = ql0 + 256
                        for c in range(16):
                            jA, jB = 2 * c, 2 * c + 1
                            kvA = 2 * (c // 4)
                            kvB = kvA + 1
                            kc2 = c // 4
                            pc = pe0 + 34 + 128 * c
                            t.wait_ge(QL.h, ql1 + 16 * (c + 1))
                            for qh in range(2):
                                t0_ = qh * TH
                                bq = pc + 64 * qh
                                ab = ac0 + 1 + 64 * c + 32 * qh

                                def emit_sc(kc):
                                    if kc == 0:
                                        if qh == 0:
                                            t.wait_ge(AC.h, ac0 + 1 + 64 * c)
                                        else:
                                            t.wait_ge(AC.h, ac0 + 1 + 64 * c + 32)
                                    else:
                                        t.wait_ge(AC.h, ab + 2 * kc)
                                    for X, (qb, psx) in enumerate(
                                            ((0, pshA), (64, pshB))):
                                        last = None
                                        for jh in range(2):
                                            last = t.matmul(
                                                psx[:, jh * 512:(jh + 1) * 512],
                                                krview(kc2)[qb:qb + 64,
                                                            kc * 128:(kc + 1) * 128],
                                                qchunk[c % 2][qb:qb + 64,
                                                              t0_ + jh * 512:
                                                              t0_ + (jh + 1) * 512],
                                                start=True, stop=True)
                                        PE.inc(last)

                                def emit_av(kc):
                                    t.wait_ge(AC.h, ab + 2 * kc + 2)
                                    if kc == 0:
                                        if qh == 0:
                                            t.wait_ge(DV.h, dv0 + 33 + 8 * c)
                                        else:
                                            t.wait_ge(DV.h, dv0 + 33 + 8 * c + 4)
                                    for X, (pavx, pbx, kv) in enumerate(
                                            ((pavA, PbA, kvA), (pavB, PbB, kvB))):
                                        last = None
                                        for jh in range(2):
                                            last = t.matmul(
                                                pavx[0:65, jh * 512:(jh + 1) * 512],
                                                vdview(kv, kc)[:, :],
                                                pbx[kc % 2][:, jh * 512:(jh + 1) * 512],
                                                start=(kc == 0), stop=(kc == 15))
                                        PE.inc(last)

                                emit_sc(0)
                                for kc in range(1, 16):
                                    emit_sc(kc)
                                    emit_av(kc - 1)
                                emit_av(15)

                    @block.scalar
                    def _(s):
                        s.wait_ge(PE.h, pe0 + 34)
                        AC.inc(s.activation(negb_all[:, :], pshA[:, 0:32], AF.Copy))
                        for c in range(16):
                            pc = pe0 + 34 + 128 * c
                            for qh in range(2):
                                bq = pc + 64 * qh
                                ab = ac0 + 1 + 64 * c + 32 * qh
                                for kc in range(16):
                                    for X, (psx, pbx, j) in enumerate(
                                            ((pshA, PbA, 2 * c), (pshB, PbB, 2 * c + 1))):
                                        if kc == 0:
                                            s.wait_ge(PE.h, bq + 1 + X)
                                        else:
                                            s.wait_ge(PE.h, bq + 4 * kc - 1 + X)
                                        AC.inc(s.activation(pbx[kc % 2][:, :],
                                                            psx[:, :], AF.Exp,
                                                            bias=negb_all[:, j:j + 1],
                                                            scale=0.125))

                    @block.gpsimd
                    def _(g):
                        for c in range(16):
                            g.wait_ge(DV.h, dv0 + 33 + 8 * c + 7)
                            QW.inc(g.dma_start(out=sall_d[2 * c:2 * c + 1, :],
                                               in_=srowA[0:1, :]), 16)
                            g.wait_ge(DV.h, dv0 + 33 + 8 * c + 8)
                            QW.inc(g.dma_start(out=sall_d[2 * c + 1:2 * c + 2, :],
                                               in_=srowB[0:1, :]), 16)

                    QL.expect(ql0, 512, "heads")
                    QW.expect(qw0, 16 * 32, "heads")
                    PE.expect(pe0, 34 + 128 * 16, "heads")
                    AC.expect(ac0, 1 + 1024, "heads")
                    DV.expect(dv0, 33 + 128, "heads")

                with nc.Block() as block, \
                     nc.psum_tensor("pSb0", [128, SEQ], F32) as pSb0, \
                     nc.psum_tensor("pSb1", [128, SEQ], F32) as pSb1:
                    pSb = [pSb0, pSb1]
                    qw0, ql0, pe0, dv0 = QW.n, QL.n, PE.n, DV.n

                    @block.gpsimd
                    def _(g):
                        QW.inc(g.dma_start(out=eselb[:, :], in_=esel_d[:, :]), 16)

                    @block.sync
                    def _(sync):
                        sync.wait_ge(QW.h, qw0)
                        QL.inc(sync.dma_start(out=SallT[:, :], in_=sall_d[:, :]), 16)

                    @block.vector
                    def _(v):
                        v.wait_ge(QL.h, ql0 + 16)
                        v.reciprocal(SinvT[:, :], SallT[:, :])
                        v.tensor_copy(Shi[:, :], SinvT[:, :])
                        DV.inc(v.scalar_tensor_tensor(Slo[:, :], Shi[:, :], -1.0,
                                                      SinvT[:, :], ALU.mult, ALU.add))
                        for c in range(16):
                            v.wait_ge(PE.h, pe0 + c + 1)
                            DV.inc(v.tensor_mul(attnview(c), attnview(c),
                                                pSb[c % 2][:, :]))

                    @block.tensor
                    def _(t):
                        t.wait_ge(QW.h, qw0 + 16)
                        for c in range(16):
                            t.wait_ge(DV.h, dv0 + 1 if c < 2 else dv0 + c)
                            last = None
                            for j in range(4):
                                sl = slice(j * 512, (j + 1) * 512)
                                t.matmul(pSb[c % 2][:, sl],
                                         eselb[:, c * 128:(c + 1) * 128],
                                         Shi[:, sl], start=True, stop=False)
                                last = t.matmul(pSb[c % 2][:, sl],
                                                eselb[:, c * 128:(c + 1) * 128],
                                                Slo[:, sl], start=False, stop=True)
                            PE.inc(last)

                    PE.expect(pe0, 16, "natt")
                    DV.expect(dv0, 17, "natt")

                with ExitStack() as octx:
                    osb = lambda n, s, d: octx.enter_context(nc.sbuf_tensor(n, s, d))
                    wsb2 = [osb(f"wout{i}", [128, SEQ], BF16) for i in range(2)]
                    r1b = [osb(f"r1b{i}", [128, SEQ], F32) for i in range(2)]
                    r2b = [osb(f"r2b{i}", [128, SEQ], F32) for i in range(2)]

                    with nc.Block() as block, \
                         nc.psum_tensor("po0", [128, SEQ], F32) as po0, \
                         nc.psum_tensor("po1", [128, SEQ], F32) as po1:
                        po = [po0, po1]
                        qw0, ql0, qs0, pe0, dv0 = QW.n, QL.n, QS.n, PE.n, DV.n

                        @block.gpsimd
                        def _(g):
                            for oc in range(NHC):
                                if oc >= 2:
                                    g.wait_ge(PE.h, pe0 + oc - 1)
                                QW.inc(g.dma_start(
                                    out=wsb2[oc % 2][:, :],
                                    in_=woutt_d[oc * 128:(oc + 1) * 128, :]), 16)

                        @block.sync
                        def _(sync):
                            for oc in range(NHC):
                                if oc >= 2:
                                    sync.wait_ge(DV.h, dv0 + oc - 1)
                                QL.inc(sync.dma_start(
                                    out=r1b[oc % 2][:, :],
                                    in_=r1_d[oc * 128:(oc + 1) * 128, :]), 16)

                        @block.tensor
                        def _(t):
                            for oc in range(NHC):
                                t.wait_ge(QW.h, qw0 + 16 * (oc + 1))
                                if oc >= 2:
                                    t.wait_ge(DV.h, dv0 + oc - 1)
                                last = None
                                for fc in range(NHC):
                                    for j in range(4):
                                        last = t.matmul(
                                            po[oc % 2][:, j * 512:(j + 1) * 512],
                                            wsb2[oc % 2][:, fc * 128:(fc + 1) * 128],
                                            attnview(fc)[:, j * 512:(j + 1) * 512],
                                            start=(fc == 0), stop=(fc == NHC - 1))
                                PE.inc(last)

                        @block.vector
                        def _(v):
                            for oc in range(NHC):
                                v.wait_ge(PE.h, pe0 + oc + 1)
                                v.wait_ge(QL.h, ql0 + 16 * (oc + 1))
                                if oc >= 2:
                                    v.wait_ge(QS.h, qs0 + 16 * (oc - 1))
                                DV.inc(v.scalar_tensor_tensor(
                                    r2b[oc % 2][:, :], po[oc % 2][:, :],
                                    sm(24 + oc), r1b[oc % 2][:, :],
                                    ALU.add, ALU.add))

                        @block.scalar
                        def _(s):
                            for oc in range(NHC):
                                s.wait_ge(DV.h, dv0 + oc + 1)
                                QS.inc(s.dma_start(
                                    out=r2_d[oc * 128:(oc + 1) * 128, :],
                                    in_=r2b[oc % 2][:, :]), 16)

                        PE.expect(pe0, NHC, "oproj")
                        DV.expect(dv0, NHC, "oproj")
                        QS.expect(qs0, 16 * NHC, "oproj")

        qs_after_attn = QS.n

        if dbg == "attn":
            with nc.Block() as block:
                @block.sync
                def _(sync):
                    sync.wait_ge(QS.h, QS.n)
                    sync.wait_ge(QW.h, QW.n)
                    sync.wait_ge(QL.h, QL.n)
            return nc

        ffn_phase(r2_d, out_d, "f2", wait_qs=qs_after_attn)

        with nc.Block() as block:
            @block.sync
            def _(sync):
                sync.wait_ge(QS.h, QS.n)
                sync.wait_ge(QW.h, QW.n)
                sync.wait_ge(QL.h, QL.n)

    return nc




def kernel(**inputs):
    x = np.asarray(inputs["L0_attn_x"], dtype=np.float32)
    consts = host_layout(inputs)
    nc = build()
    in_maps = []
    for b in range(8):
        m = dict(consts)
        m["xT"] = np.ascontiguousarray(x[b].T)
        in_maps.append(m)
    res = run_bass_kernel_spmd(nc, in_maps, core_ids=list(range(8)))
    out = np.stack([np.asarray(res.results[b]["outT"]).T for b in range(8)], axis=0)
    return np.ascontiguousarray(out.astype(np.float32))


# version 11
# speedup vs baseline: 1.0551x; 1.0551x over previous
import numpy as np
import ml_dtypes
from contextlib import ExitStack

import concourse.bass as bass
import concourse.mybir as mybir
from concourse.bass_utils import run_bass_kernel_spmd

F32 = mybir.dt.float32
BF16 = mybir.dt.bfloat16
I32 = mybir.dt.int32
AF = mybir.ActivationFunctionType
ALU = mybir.AluOpType
AX = mybir.AxisListType

SEQ = 2048
DM = 2048
DI = 8192
TH = 1024
NHC = 16
NIC = 64
NGRP = 4
ICG = NIC // NGRP
NQH = 32
NKV = 8
DH = 64
EPS = 1e-5
INV_DM = 1.0 / 2048.0

BF = ml_dtypes.bfloat16



def _qhead_pairs():
    return [(8 * (c // 4) + (c % 4), 8 * (c // 4) + (c % 4) + 4) for c in range(16)]


def _rope_rows(h):
    base = h * DH
    return [base + 2 * i for i in range(32)] + [base + 2 * i + 1 for i in range(32)]


def _tile4(w, n0, n1):
    return np.ascontiguousarray(w.reshape(n0, 128, n1, 128).transpose(0, 3, 2, 1))


def _rope_tables():
    d = np.arange(0, DH, 2, dtype=np.float32)
    freq = (1.0 / (10000.0 ** (d / np.float32(DH)))).astype(np.float32)
    pos = np.arange(SEQ, dtype=np.float64)
    theta = pos[None, :] * freq[:, None].astype(np.float64)
    cosF = np.tile(np.cos(theta), (4, 1)).astype(np.float32)
    sinF = np.tile(np.sin(theta), (4, 1))
    sign = np.ones((128, 1))
    sign[32:64] = -1.0
    sign[96:128] = -1.0
    return cosF, (sinF * sign).astype(np.float32)


def host_layout(inp):
    pairs = _qhead_pairs()
    qperm = np.array([r for h0, h1 in pairs for r in _rope_rows(h0) + _rope_rows(h1)])
    kperm = np.array([r for k in range(NKV) for r in _rope_rows(k)])

    wq = np.asarray(inp["L0_attn_W_q"], dtype=np.float32)[qperm]
    wkv_in = np.asarray(inp["L0_attn_W_kv"], dtype=np.float32)
    wkv = wkv_in.copy()
    wkv[0:512] = wkv_in[0:512][kperm]
    fperm = np.zeros(DM, dtype=np.int64)
    for c in range(16):
        h0, h1 = pairs[c]
        fperm[c * 128: c * 128 + 64] = np.arange(64) + h0 * 64
        fperm[c * 128 + 64: (c + 1) * 128] = np.arange(64) + h1 * 64
    wout = np.asarray(inp["L0_attn_W_out"], dtype=np.float32)[:, fperm]

    bq = np.asarray(inp["L0_attn_b_q"], dtype=np.float32)[qperm].reshape(16, 128).T
    bkv_in = np.asarray(inp["L0_attn_b_kv"], dtype=np.float32)
    bkv = bkv_in.copy()
    bkv[0:512] = bkv_in[0:512][kperm]
    bkv = bkv.reshape(8, 128).T
    bout = np.asarray(inp["L0_attn_b_out"], dtype=np.float32).reshape(16, 128).T
    sg = np.asarray(inp["L0_ffn_gate_scale"], dtype=np.float32).reshape(64, 128).T
    su = np.asarray(inp["L0_ffn_up_scale"], dtype=np.float32).reshape(64, 128).T
    sd = np.asarray(inp["L0_ffn_down_scale"], dtype=np.float32).reshape(16, 128).T
    smalls = np.ascontiguousarray(
        np.concatenate([bq, bkv, bout, sg, su, sd], axis=1).astype(np.float32))

    gw = _tile4(np.asarray(inp["L0_ffn_gate_w"], dtype=np.int32), NIC, NHC)
    uw = _tile4(np.asarray(inp["L0_ffn_up_w"], dtype=np.int32), NIC, NHC)
    dw = _tile4(np.asarray(inp["L0_ffn_down_w"], dtype=np.int32), NHC, NIC)
    dw = np.ascontiguousarray(
        dw.reshape(NHC, 128, NGRP, ICG, 128).transpose(2, 0, 1, 3, 4))

    cosF, sinT = _rope_tables()
    esel = np.zeros((32, 16 * 128), dtype=np.float32)
    for c in range(16):
        esel[2 * c, c * 128: c * 128 + 64] = 1.0
        esel[2 * c + 1, c * 128 + 64: (c + 1) * 128] = 1.0

    return dict(
        smalls=smalls,
        gwt=np.ascontiguousarray(gw.reshape(DI, DM)),
        uwt=np.ascontiguousarray(uw.reshape(DI, DM)),
        dwt=np.ascontiguousarray(dw.reshape(NGRP * NHC * 128, ICG * 128)),
        wqt=np.ascontiguousarray(_tile4(wq, 16, 16).reshape(DM, DM)),
        wkvt=np.ascontiguousarray(_tile4(wkv, 8, 16).reshape(1024, DM)),
        woutt=np.ascontiguousarray(_tile4(wout, 16, 16).reshape(DM, DM)),
        cosF=cosF, sinT=sinT, esel=esel,
        ident=np.eye(128, dtype=np.float32),
    )




class Sem:
    def __init__(self, h):
        self.h = h
        self.n = 0

    def inc(self, instr, k=1):
        instr.then_inc(self.h, k)
        self.n += k
        return self.n

    def expect(self, base, add, tag):
        assert self.n == base + add, f"{tag}: sem {self.h.name} n={self.n} != {base}+{add}"


def build(dbg=None):
    nc = bass.Bass("TRN2", target_bir_lowering=False)

    xT = nc.dram_tensor("xT", [DM, SEQ], F32, kind="ExternalInput")
    gwt_d = nc.dram_tensor("gwt", [DI, DM], I32, kind="ExternalInput")
    uwt_d = nc.dram_tensor("uwt", [DI, DM], I32, kind="ExternalInput")
    dwt_d = nc.dram_tensor("dwt", [NGRP * NHC * 128, ICG * 128], I32, kind="ExternalInput")
    wqt_d = nc.dram_tensor("wqt", [DM, DM], F32, kind="ExternalInput")
    wkvt_d = nc.dram_tensor("wkvt", [1024, DM], F32, kind="ExternalInput")
    woutt_d = nc.dram_tensor("woutt", [DM, DM], F32, kind="ExternalInput")
    smalls_d = nc.dram_tensor("smalls", [128, 184], F32, kind="ExternalInput")
    cosF_d = nc.dram_tensor("cosF", [128, SEQ], F32, kind="ExternalInput")
    sinT_d = nc.dram_tensor("sinT", [128, SEQ], F32, kind="ExternalInput")
    esel_d = nc.dram_tensor("esel", [32, 16 * 128], F32, kind="ExternalInput")
    ident_d = nc.dram_tensor("ident", [128, 128], F32, kind="ExternalInput")
    out_d = nc.dram_tensor("outT", [DM, SEQ], F32, kind="ExternalOutput")

    rk1 = "ExternalOutput" if dbg in ("f1", "attn") else "Internal"
    rk2 = "ExternalOutput" if dbg == "attn" else "Internal"
    r1_d = nc.dram_tensor("r1", [DM, SEQ], F32, kind=rk1)
    r2_d = nc.dram_tensor("r2", [DM, SEQ], F32, kind=rk2)
    sall_d = nc.dram_tensor("sall", [32, SEQ], F32, kind="Internal")
    qr_d = nc.dram_tensor("qr", [DM, SEQ], BF16, kind="Internal")

    ctx = ExitStack()
    with ctx:
        QW = Sem(ctx.enter_context(nc.semaphore("QW")))
        QL = Sem(ctx.enter_context(nc.semaphore("QL")))
        QS = Sem(ctx.enter_context(nc.semaphore("QS")))
        PE = Sem(ctx.enter_context(nc.semaphore("PE")))
        AC = Sem(ctx.enter_context(nc.semaphore("AC")))
        DV = Sem(ctx.enter_context(nc.semaphore("DV")))

        sbg = lambda n, s, d: ctx.enter_context(nc.sbuf_tensor(n, s, d))
        ones128 = sbg("ones128", [128, 1], BF16)
        onesg = sbg("onesg", [128, 128], BF16)
        onesrow = onesg[0:1, :]
        onesrow64 = onesg[64:65, :]
        neg128row = sbg("neg128row", [1, 128], BF16)
        identb = sbg("identb", [128, 128], BF16)
        epst = sbg("epst", [128, 1], F32)
        smalls = sbg("smallst", [128, 184], F32)
        halfsd = sbg("halfsd", [128, 16], F32)

        def sm(col):
            return smalls[:, col: col + 1]

        with nc.Block() as block:
            @block.sync
            def _(sync):
                QL.inc(sync.dma_start(out=smalls[:, :], in_=smalls_d[:, :]), 16)

            @block.gpsimd
            def _(g):
                QW.inc(g.dma_start(out=identb[:, :], in_=ident_d[:, :]), 16)

            @block.vector
            def _(v):
                v.wait_ge(QL.h, 16)
                v.memset(ones128[:, :], 1.0)
                v.memset(onesg[:, :], 1.0)
                v.memset(neg128row[:, :], -128.0)
                v.memset(epst[:, :], EPS)
                v.tensor_scalar_mul(halfsd[:, :], smalls[:, 168:184], 0.5)

        def ln_full(tag, src, xview, lnb, wait_qs=None):
            (xc, xq, xqs, mb, rb, tbuf, mrow, trow, rstd, rowsb) = lnb

            def RB(j):
                return rowsb[j][0:1, :]

            with nc.Block() as block:
                ql0, pe0, ac0, dv0 = QL.n, PE.n, AC.n, DV.n

                @block.sync
                def _(sync):
                    if wait_qs is not None:
                        sync.wait_ge(QS.h, wait_qs)
                    for dc in range(NHC):
                        if dc >= 2:
                            sync.wait_ge(DV.h, dv0 + 2 * dc - 3)
                        QL.inc(sync.dma_start(
                            out=xc[dc % 2][:, :],
                            in_=src[dc * 128:(dc + 1) * 128, :]), 16)

                with nc.psum_tensor(f"{tag}psx", [1, SEQ], F32) as psx, \
                     nc.psum_tensor(f"{tag}psq", [1, SEQ], F32) as psq:

                    @block.tensor
                    def _(t):
                        for dc in range(NHC):
                            t.wait_ge(DV.h, dv0 + 2 * dc + 1)
                            for j in range(4):
                                t.matmul(psx[0:1, j * 512:(j + 1) * 512],
                                         ones128[:, :],
                                         xq[dc % 2][:, j * 512:(j + 1) * 512],
                                         start=(dc == 0), stop=(dc == NHC - 1))
                            t.wait_ge(DV.h, dv0 + 2 * dc + 2)
                            last = None
                            for j in range(4):
                                last = t.matmul(psq[0:1, j * 512:(j + 1) * 512],
                                                ones128[:, :],
                                                xqs[dc % 2][:, j * 512:(j + 1) * 512],
                                                start=(dc == 0), stop=(dc == NHC - 1))
                            PE.inc(last)

                    @block.scalar
                    def _(s):
                        s.wait_ge(DV.h, dv0 + 2 * NHC + 1)
                        s.activation(trow[0:1, :], trow[0:1, :], AF.Ln,
                                     bias=epst[0:1, 0:1])
                        AC.inc(s.activation(rstd[0:1, :], trow[0:1, :], AF.Exp,
                                            scale=-0.5))

                    @block.vector
                    def _(v):
                        for dc in range(NHC):
                            v.wait_ge(QL.h, ql0 + 16 * (dc + 1))
                            if dc >= 2:
                                v.wait_ge(PE.h, pe0 + dc - 1)
                            DV.inc(v.tensor_copy(xq[dc % 2][:, :], xc[dc % 2][:, :]))
                            DV.inc(v.tensor_mul(xqs[dc % 2][:, :], xq[dc % 2][:, :],
                                                xq[dc % 2][:, :]))
                        v.wait_ge(PE.h, pe0 + NHC)
                        v.wait_ge(AC.h, ac0)
                        v.tensor_scalar_mul(mrow[0:1, :], psx[0:1, :], INV_DM)
                        v.tensor_scalar_mul(trow[0:1, :], psq[0:1, :], INV_DM)
                        v.tensor_mul(tbuf[64:65, 0:SEQ], mrow[0:1, :], mrow[0:1, :])
                        DV.inc(v.tensor_sub(trow[0:1, :], trow[0:1, :],
                                            tbuf[64:65, 0:SEQ]))
                        v.wait_ge(AC.h, ac0 + 1)
                        v.tensor_copy(RB(0), mrow[0:1, :])
                        DV.inc(v.tensor_copy(RB(1), rstd[0:1, :]))

                with nc.psum_tensor(f"{tag}pmb", [128, SEQ], F32) as pmb, \
                     nc.psum_tensor(f"{tag}prb", [128, SEQ], F32) as prb:

                    @block.tensor
                    def _(t):
                        t.wait_ge(DV.h, dv0 + 2 * NHC + 2)
                        for j in range(4):
                            sl = slice(j * 512, (j + 1) * 512)
                            t.matmul(pmb[:, sl], onesrow[:, :], RB(0)[:, sl],
                                     start=True, stop=True)
                            last = t.matmul(prb[:, sl], onesrow[:, :], RB(1)[:, sl],
                                            start=True, stop=True)
                        PE.inc(last)

                    @block.scalar
                    def _(s):
                        s.wait_ge(PE.h, pe0 + NHC + 1)
                        s.activation(mb[:, :], pmb[:, :], AF.Copy)
                        AC.inc(s.activation(rb[:, :], prb[:, :], AF.Copy))

                QL.expect(ql0, 256, tag)
                PE.expect(pe0, NHC + 1, tag)
                AC.expect(ac0, 2, tag)
                DV.expect(dv0, 2 * NHC + 2, tag)

            with nc.Block() as block:
                ql0, dv0 = QL.n, DV.n

                @block.sync
                def _(sync):
                    for dc in range(NHC):
                        if dc >= 2:
                            sync.wait_ge(DV.h, dv0 + 2 * dc - 2)
                        QL.inc(sync.dma_start(
                            out=xc[dc % 2][:, :],
                            in_=src[dc * 128:(dc + 1) * 128, :]), 16)

                @block.vector
                def _(v):
                    for dc in range(NHC):
                        v.wait_ge(QL.h, ql0 + 16 * (dc + 1))
                        DV.inc(v.tensor_sub(tbuf[:, :], xc[dc % 2][:, :], mb[:, :]))
                        DV.inc(v.tensor_mul(xview(dc), tbuf[:, :], rb[:, :]))

                QL.expect(ql0, 256, tag + "n")
                DV.expect(dv0, 32, tag + "n")

        def mk_lnb(pfx, c):
            lsb = lambda n, s, d: c.enter_context(nc.sbuf_tensor(pfx + n, s, d))
            prow1 = lsb("prow1", [128, SEQ], F32)
            rowb1 = lsb("rowb1", [1, SEQ], BF16)
            rowb2 = lsb("rowb2", [1, SEQ], BF16)
            tbuf = lsb("tbuf", [128, SEQ], F32)
            return (
                [lsb(f"xc{i}", [128, SEQ], F32) for i in range(2)],
                [lsb(f"xq{i}", [128, SEQ], BF16) for i in range(2)],
                [lsb(f"xqs{i}", [128, SEQ], BF16) for i in range(2)],
                lsb("mb", [128, SEQ], F32),
                lsb("rb", [128, SEQ], F32),
                tbuf,
                prow1[0:1, :],
                prow1[64:65, :],
                tbuf[64:65, :],
                (rowb1, rowb2),
            )

        def ffn_phase(src, dst, tag, wait_qs):
            fctx = ExitStack()
            with fctx:
                fsb = lambda n, s, d: fctx.enter_context(nc.sbuf_tensor(f"{tag}{n}", s, d))
                xhat = fsb("xhat", [128, NHC * SEQ], BF16)
                hbuf = fsb("h", [128, ICG * SEQ], BF16)

                def xview(dc):
                    return xhat[:, dc * SEQ:(dc + 1) * SEQ]

                def hview(icg):
                    return hbuf[:, icg * SEQ:(icg + 1) * SEQ]

                with ExitStack() as lctx:
                    lnb = mk_lnb(f"{tag}l", lctx)
                    ln_full(f"{tag}l", src, xview, lnb, wait_qs)

                with ExitStack() as wctx:
                    wsb_ = lambda n, s, d: wctx.enter_context(
                        nc.sbuf_tensor(f"{tag}{n}", s, d))
                    gsb = [wsb_(f"gwb{i}", [128, SEQ], BF16) for i in range(2)]
                    usb = [wsb_(f"uwb{i}", [128, SEQ], BF16) for i in range(2)]
                    dsb = [wsb_(f"dwb{i}", [128, SEQ], BF16) for i in range(2)]
                    gs_buf = wsb_("gs", [128, SEQ], BF16)
                    us_buf = wsb_("us", [128, SEQ], BF16)
                    rsrc = [wsb_(f"rs{i}", [128, SEQ], F32) for i in range(2)]
                    rout = [wsb_(f"ro{i}", [128, SEQ], F32) for i in range(2)]
                    pf = {"qw": None}

                    for grp in range(NGRP):
                        with nc.Block() as block, \
                             nc.psum_tensor(f"{tag}{grp}pg", [128, SEQ], F32) as pg, \
                             nc.psum_tensor(f"{tag}{grp}pu", [128, SEQ], F32) as pu:
                            qw0, pe0, ac0, dv0 = QW.n, PE.n, AC.n, DV.n

                            def dv_c(i):
                                return dv0 + (1 if i == 0 else 2 * i)

                            def dv_h(i):
                                return dv0 + (2 * i + 3 if i <= ICG - 2 else 2 * ICG)

                            pfq = pf["qw"]
                            pf["qw"] = None

                            def qw_land(icg):
                                if pfq is not None:
                                    return pfq[icg] if icg < 2 else qw0 + 32 * (icg - 1)
                                return qw0 + 32 * (icg + 1)

                            @block.gpsimd
                            def _(g):
                                for icg in range(ICG):
                                    gic = grp * ICG + icg
                                    if pfq is not None and icg < 2:
                                        continue
                                    if icg >= 2:
                                        g.wait_ge(PE.h, pe0 + 2 * icg - 2)
                                    QW.inc(g.dma_start(
                                        out=gsb[icg % 2][:, :],
                                        in_=gwt_d[gic * 128:(gic + 1) * 128, :]), 16)
                                    QW.inc(g.dma_start(
                                        out=usb[icg % 2][:, :],
                                        in_=uwt_d[gic * 128:(gic + 1) * 128, :]), 16)

                            @block.tensor
                            def _(t):
                                for icg in range(ICG):
                                    t.wait_ge(DV.h, dv_c(icg))
                                    if icg >= 1:
                                        t.wait_ge(AC.h, ac0 + 2 * icg - 1)
                                    last = None
                                    for hc in range(NHC):
                                        for j in range(4):
                                            last = t.matmul(
                                                pg[:, j * 512:(j + 1) * 512],
                                                gsb[icg % 2][:, hc * 128:(hc + 1) * 128],
                                                xview(hc)[:, j * 512:(j + 1) * 512],
                                                start=(hc == 0), stop=(hc == NHC - 1))
                                    PE.inc(last)
                                    if icg >= 1:
                                        t.wait_ge(AC.h, ac0 + 2 * icg)
                                    last = None
                                    for hc in range(NHC):
                                        for j in range(4):
                                            last = t.matmul(
                                                pu[:, j * 512:(j + 1) * 512],
                                                usb[icg % 2][:, hc * 128:(hc + 1) * 128],
                                                xview(hc)[:, j * 512:(j + 1) * 512],
                                                start=(hc == 0), stop=(hc == NHC - 1))
                                    PE.inc(last)

                            @block.scalar
                            def _(s):
                                for icg in range(ICG):
                                    gic = grp * ICG + icg
                                    s.wait_ge(PE.h, pe0 + 2 * icg + 1)
                                    if icg >= 1:
                                        s.wait_ge(DV.h, dv_h(icg - 1))
                                    AC.inc(s.activation(gs_buf[:, :], pg[:, :],
                                                        AF.Silu, scale=sm(40 + gic)))
                                    s.wait_ge(PE.h, pe0 + 2 * icg + 2)
                                    AC.inc(s.activation(us_buf[:, :], pu[:, :],
                                                        AF.Identity,
                                                        scale=sm(104 + gic)))

                            @block.vector
                            def _(v):
                                for icg in range(ICG):
                                    v.wait_ge(QW.h, qw_land(icg))
                                    v.tensor_scalar_add(gsb[icg % 2][:, :],
                                                        gsb[icg % 2][:, :], -128.0)
                                    DV.inc(v.tensor_scalar_add(
                                        usb[icg % 2][:, :], usb[icg % 2][:, :],
                                        -128.0))
                                    if icg >= 1:
                                        v.wait_ge(AC.h, ac0 + 2 * icg)
                                        DV.inc(v.tensor_mul(hview(icg - 1),
                                                            gs_buf[:, :],
                                                            us_buf[:, :]))
                                v.wait_ge(AC.h, ac0 + 2 * ICG)
                                DV.inc(v.tensor_mul(hview(ICG - 1), gs_buf[:, :],
                                                    us_buf[:, :]))

                            QW.expect(qw0, 32 * ICG - (64 if pfq is not None else 0),
                                      tag + "gu")
                            PE.expect(pe0, 2 * ICG, tag + "gu")
                            AC.expect(ac0, 2 * ICG, tag + "gu")
                            DV.expect(dv0, 2 * ICG, tag + "gu")

                        with nc.Block() as block, \
                             nc.psum_tensor(f"{tag}{grp}pd0", [128, SEQ], F32) as pd0, \
                             nc.psum_tensor(f"{tag}{grp}pd1", [128, SEQ], F32) as pd1:
                            pd = [pd0, pd1]
                            qw0, ql0, qs0 = QW.n, QL.n, QS.n
                            pe0, ac0, dv0 = PE.n, AC.n, DV.n
                            rsrc_t = src if grp == 0 else dst

                            def dv_c(i):
                                return dv0 + (1 if i == 0 else 2 * i)

                            def dv_stt(i):
                                return dv0 + (2 * i + 3 if i <= NHC - 2 else 2 * NHC)

                            @block.gpsimd
                            def _(g):
                                for hc in range(NHC):
                                    if hc >= 2:
                                        g.wait_ge(PE.h, pe0 + hc - 1)
                                    r0 = (grp * NHC + hc) * 128
                                    QW.inc(g.dma_start(
                                        out=dsb[hc % 2][:, :],
                                        in_=dwt_d[r0:r0 + 128, :]), 16)
                                if grp < NGRP - 1:
                                    qwl = []
                                    for icg in range(2):
                                        gic = (grp + 1) * ICG + icg
                                        QW.inc(g.dma_start(
                                            out=gsb[icg][:, :],
                                            in_=gwt_d[gic * 128:(gic + 1) * 128, :]), 16)
                                        qwl.append(QW.inc(g.dma_start(
                                            out=usb[icg][:, :],
                                            in_=uwt_d[gic * 128:(gic + 1) * 128, :]), 16))
                                    pf["qw"] = qwl

                            @block.sync
                            def _(sync):
                                for hc in range(NHC):
                                    if hc == 0 and grp > 0:
                                        sync.wait_ge(QS.h, qs0)
                                    if hc >= 2:
                                        sync.wait_ge(DV.h, dv0 + 2 * hc - 1)
                                    QL.inc(sync.dma_start(
                                        out=rsrc[hc % 2][:, :],
                                        in_=rsrc_t[hc * 128:(hc + 1) * 128, :]), 16)

                            @block.vector
                            def _(v):
                                for hc in range(NHC):
                                    v.wait_ge(QW.h, qw0 + 16 * (hc + 1))
                                    DV.inc(v.tensor_scalar_add(
                                        dsb[hc % 2][:, :], dsb[hc % 2][:, :],
                                        -128.0))
                                    if hc >= 1:
                                        v.wait_ge(PE.h, pe0 + hc)
                                        v.wait_ge(QL.h, ql0 + 16 * hc)
                                        if hc >= 3:
                                            v.wait_ge(QS.h, qs0 + 16 * (hc - 2))
                                        DV.inc(v.scalar_tensor_tensor(
                                            rout[(hc - 1) % 2][:, :],
                                            pd[(hc - 1) % 2][:, :],
                                            halfsd[:, hc - 1:hc],
                                            rsrc[(hc - 1) % 2][:, :],
                                            ALU.mult, ALU.add))
                                v.wait_ge(PE.h, pe0 + NHC)
                                v.wait_ge(QL.h, ql0 + 16 * NHC)
                                v.wait_ge(QS.h, qs0 + 16 * (NHC - 2))
                                DV.inc(v.scalar_tensor_tensor(
                                    rout[(NHC - 1) % 2][:, :],
                                    pd[(NHC - 1) % 2][:, :],
                                    halfsd[:, NHC - 1:NHC],
                                    rsrc[(NHC - 1) % 2][:, :],
                                    ALU.mult, ALU.add))

                            @block.tensor
                            def _(t):
                                for hc in range(NHC):
                                    t.wait_ge(DV.h, dv_c(hc))
                                    if hc >= 2:
                                        t.wait_ge(DV.h, dv0 + 2 * hc - 1)
                                    last = None
                                    for icg in range(ICG):
                                        for j in range(4):
                                            last = t.matmul(
                                                pd[hc % 2][:, j * 512:(j + 1) * 512],
                                                dsb[hc % 2][:, icg * 128:(icg + 1) * 128],
                                                hview(icg)[:, j * 512:(j + 1) * 512],
                                                start=(icg == 0), stop=(icg == ICG - 1))
                                    PE.inc(last)

                            @block.scalar
                            def _(s):
                                for hc in range(NHC):
                                    s.wait_ge(DV.h, dv_stt(hc))
                                    QS.inc(s.dma_start(
                                        out=dst[hc * 128:(hc + 1) * 128, :],
                                        in_=rout[hc % 2][:, :]), 16)

                            QW.expect(qw0, 16 * NHC + (64 if grp < NGRP - 1 else 0),
                                      tag + "dn")
                            QL.expect(ql0, 16 * NHC, tag + "dn")
                            QS.expect(qs0, 16 * NHC, tag + "dn")
                            PE.expect(pe0, NHC, tag + "dn")
                            DV.expect(dv0, 2 * NHC, tag + "dn")
            return

        ffn_phase(xT, r1_d, "f1", wait_qs=None)
        qs_after_f1 = QS.n

        if dbg == "f1":
            with nc.Block() as block:
                @block.sync
                def _(sync):
                    sync.wait_ge(QS.h, QS.n)
                    sync.wait_ge(QW.h, QW.n)
                    sync.wait_ge(QL.h, QL.n)
            return nc

        actx = ExitStack()
        with actx:
            asb = lambda n, s, d: actx.enter_context(nc.sbuf_tensor(n, s, d))
            Kr = asb("Kr", [128, 4 * SEQ], BF16)
            Vd = asb("Vd", [128, 8 * 16 * 65], BF16)

            def krview(kc2):
                return Kr[:, kc2 * SEQ:(kc2 + 1) * SEQ]

            def vdview(kv, kc):
                return Vd[:, kv * 1040 + kc * 65: kv * 1040 + (kc + 1) * 65]

            with ExitStack() as sctx:
                ssb = lambda n, s, d: sctx.enter_context(nc.sbuf_tensor(n, s, d))
                xhat2 = ssb("xhat2", [128, NHC * SEQ], BF16)

                def x2view(dc):
                    return xhat2[:, dc * SEQ:(dc + 1) * SEQ]

                with ExitStack() as lctx:
                    lnb = mk_lnb("a_l", lctx)
                    ln_full("a_l", r1_d, x2view, lnb, wait_qs=qs_after_f1)

                cosT = ssb("cosT", [128, SEQ], BF16)
                sinTt = ssb("sinTt", [128, SEQ], BF16)
                wsb = [ssb(f"wqkv{i}", [128, SEQ], BF16) for i in range(2)]
                qbf = ssb("qbf", [128, SEQ], BF16)
                rot2 = ssb("rot2", [128, SEQ], BF16)
                rotq = [ssb(f"rotq{i}", [128, SEQ], BF16) for i in range(2)]
                vtmp = [ssb(f"vtmp{i}", [128, SEQ], BF16) for i in range(4)]

                with nc.Block() as block:
                    @block.gpsimd
                    def _(g):
                        QW.inc(g.dma_start(out=cosT[:, :], in_=cosF_d[:, :]), 16)
                        QW.inc(g.dma_start(out=sinTt[:, :], in_=sinT_d[:, :]), 16)

                with nc.Block() as block, \
                     nc.psum_tensor("pqkv0", [128, SEQ], F32) as pq0, \
                     nc.psum_tensor("pqkv1", [128, SEQ], F32) as pq1:
                    pq = [pq0, pq1]
                    qw0, qs0, pe0, ac0, dv0 = QW.n, QS.n, PE.n, AC.n, DV.n
                    NOC = 24

                    def isV(oc):
                        return oc >= 20

                    def dv_rope(oc):
                        return dv0 + sum(1 for o in range(oc + 1) if not isV(o))

                    @block.gpsimd
                    def _(g):
                        for oc in range(NOC):
                            if oc >= 2:
                                g.wait_ge(PE.h, pe0 + oc - 1)
                            src_w = wqt_d if oc < 16 else wkvt_d
                            r0 = oc * 128 if oc < 16 else (oc - 16) * 128
                            QW.inc(g.dma_start(out=wsb[oc % 2][:, :],
                                               in_=src_w[r0:r0 + 128, :]), 16)

                    @block.tensor
                    def _(t):
                        for oc in range(NOC):
                            t.wait_ge(QW.h, qw0 + 16 * (oc + 1))
                            if oc >= 2:
                                t.wait_ge(AC.h, ac0 + oc - 1)
                            last = None
                            for dc in range(NHC):
                                for j in range(4):
                                    last = t.matmul(
                                        pq[oc % 2][:, j * 512:(j + 1) * 512],
                                        wsb[oc % 2][:, dc * 128:(dc + 1) * 128],
                                        xhat2[:, dc * SEQ + j * 512: dc * SEQ + (j + 1) * 512],
                                        start=(dc == 0), stop=(dc == NHC - 1))
                            PE.inc(last)

                    @block.scalar
                    def _(s):
                        for oc in range(NOC):
                            s.wait_ge(PE.h, pe0 + oc + 1)
                            if oc < 20:
                                if oc >= 1 and not isV(oc - 1):
                                    nqk = sum(1 for o in range(oc) if not isV(o))
                                    s.wait_ge(DV.h, dv0 + nqk)
                                AC.inc(s.activation(qbf[:, :], pq[oc % 2][:, :],
                                                    AF.Identity, bias=sm(oc)))
                            else:
                                AC.inc(s.activation(vtmp[oc - 20][:, :],
                                                    pq[oc % 2][:, :],
                                                    AF.Identity, bias=sm(oc)))
                            if 1 <= oc <= 16 and (oc - 1) < 16:
                                s.wait_ge(DV.h, dv_rope(oc - 1))
                                QS.inc(s.dma_start(
                                    out=qr_d[(oc - 1) * 128: oc * 128, :],
                                    in_=rotq[(oc - 1) % 2][:, :]), 16)

                    @block.vector
                    def _(v):
                        for kv in range(8):
                            for kc in range(16):
                                v.memset(vdview(kv, kc)[:, 64:65], 1.0)
                        for oc in range(20):
                            v.wait_ge(AC.h, ac0 + oc + 1)
                            if oc < 16 and oc >= 2:
                                v.wait_ge(QS.h, qs0 + 16 * (oc - 1))
                            for (d0, s0) in ((0, 32), (32, 0), (64, 96), (96, 64)):
                                v.tensor_mul(rot2[d0:d0 + 32, :],
                                             qbf[s0:s0 + 32, :], sinTt[s0:s0 + 32, :])
                            v.tensor_mul(qbf[:, :], qbf[:, :], cosT[:, :])
                            dest = rotq[oc % 2][:, :] if oc < 16 \
                                else krview(oc - 16)
                            DV.inc(v.tensor_add(dest, qbf[:, :], rot2[:, :]))

                    QW.expect(qw0, 16 * NOC, "qkv")
                    PE.expect(pe0, NOC, "qkv")
                    AC.expect(ac0, NOC, "qkv")
                    DV.expect(dv0, 20, "qkv")
                    QS.expect(qs0, 16 * 16, "qkv")

                with nc.Block() as block, \
                     nc.psum_tensor("ptv0", [128, 64], BF16) as ptv0, \
                     nc.psum_tensor("ptv1", [128, 64], BF16) as ptv1:
                    ptv = [ptv0, ptv1]
                    pe0, ac0 = PE.n, AC.n

                    @block.tensor
                    def _(t):
                        for i in range(128):
                            kvh = i // 16
                            kc = i % 16
                            b = (kvh % 2) * 64
                            if i >= 2:
                                t.wait_ge(AC.h, ac0 + i - 1)
                            PE.inc(t.transpose(
                                ptv[i % 2][:, :],
                                vtmp[kvh // 2][b:b + 64, kc * 128:(kc + 1) * 128],
                                identb[b:b + 64, b:b + 64]))

                    @block.scalar
                    def _(s):
                        for i in range(128):
                            kvh = i // 16
                            kc = i % 16
                            s.wait_ge(PE.h, pe0 + i + 1)
                            AC.inc(s.activation(vdview(kvh, kc)[:, 0:64],
                                                ptv[i % 2][:, :], AF.Copy))

                    PE.expect(pe0, 128, "vt")
                    AC.expect(ac0, 128, "vt")

            qs_after_qkv = QS.n

            with ExitStack() as sctx:
                ssb = lambda n, s, d: sctx.enter_context(nc.sbuf_tensor(n, s, d))
                attn = ssb("attn", [128, NHC * SEQ], BF16)
                PbA = [ssb(f"PbA{i}", [128, TH], BF16) for i in range(2)]
                PbB = [ssb(f"PbB{i}", [128, TH], BF16) for i in range(2)]
                qchunk = [ssb(f"qch{i}", [128, SEQ], BF16) for i in range(2)]
                kbars = ssb("kbars", [128, 4], BF16)
                kbf32 = ssb("kbf32", [128, 4], F32)
                negb_all = ssb("negball", [128, 32], F32)
                statr = ssb("statr", [1, 64], F32)
                statb = ssb("statb", [1, 32], BF16)
                srowA = ssb("srowA", [1, SEQ], F32)
                srowB = ssb("srowB", [1, SEQ], F32)
                SallT = ssb("SallT", [32, SEQ], F32)
                SinvT = ssb("SinvT", [32, SEQ], F32)
                Shi = ssb("Shi", [32, SEQ], BF16)
                Slo = ssb("Slo", [32, SEQ], BF16)
                eselb = ssb("eselb", [32, 16 * 128], BF16)

                def attnview(c):
                    return attn[:, c * SEQ:(c + 1) * SEQ]

                with nc.Block() as block, \
                     nc.psum_tensor("pshA", [128, TH], F32) as pshA, \
                     nc.psum_tensor("pshB", [128, TH], F32) as pshB, \
                     nc.psum_tensor("pavA", [65, TH], F32) as pavA, \
                     nc.psum_tensor("pavB", [65, TH], F32) as pavB:
                    qw0, ql0, qs0, pe0, ac0, dv0 = QW.n, QL.n, QS.n, PE.n, AC.n, DV.n

                    @block.sync
                    def _(sync):
                        for c in range(16):
                            if c == 0:
                                sync.wait_ge(QS.h, qs0)
                            if c >= 2:
                                sync.wait_ge(PE.h, pe0 + 2 * c - 2)
                            QL.inc(sync.dma_start(
                                out=qchunk[c % 2][:, :],
                                in_=qr_d[c * 128:(c + 1) * 128, :]), 16)
                        for c in range(16):
                            sync.wait_ge(PE.h, pe0 + 34 if c < 2
                                         else pe0 + 34 + 128 * (c - 1))
                            QL.inc(sync.dma_start(
                                out=qchunk[c % 2][:, :],
                                in_=qr_d[c * 128:(c + 1) * 128, :]), 16)

                    @block.vector
                    def _(v):
                        for kc2 in range(4):
                            v.reduce_sum(kbf32[:, kc2:kc2 + 1],
                                         krview(kc2), axis=AX.X)
                        last = None
                        for kc2 in range(4):
                            last = v.tensor_scalar_mul(kbars[:, kc2:kc2 + 1],
                                                       kbf32[:, kc2:kc2 + 1], INV_DM)
                        DV.inc(last)
                        for j in range(NQH):
                            r = (j % 2) * 32
                            v.wait_ge(PE.h, pe0 + j + 1)
                            v.reduce_max(statr[0:1, 0:1], pshA[r:r + 1, 0:TH], axis=AX.X)
                            v.reduce_max(statr[0:1, 1:2], pshB[r:r + 1, 0:TH], axis=AX.X)
                            v.tensor_reduce(statr[0:1, 2:3], statr[0:1, 0:2],
                                            AX.X, ALU.max, negate=True)
                            DV.inc(v.tensor_scalar_mul(statb[0:1, j:j + 1],
                                                       statr[0:1, 2:3], 0.125))
                        for c in range(16):
                            pc = pe0 + 34 + 128 * c
                            for qh in range(2):
                                t0_ = qh * TH
                                v.wait_ge(PE.h, pc + 64 * qh + 63)
                                DV.inc(v.tensor_copy(
                                    attnview(c)[0:64, t0_:t0_ + TH], pavA[0:64, :]))
                                v.wait_ge(PE.h, pc + 64 * qh + 64)
                                DV.inc(v.tensor_copy(
                                    attnview(c)[64:128, t0_:t0_ + TH], pavB[0:64, :]))
                                if qh == 0 and c >= 1:
                                    v.wait_ge(QW.h, qw0 + 32 * c - 16)
                                DV.inc(v.tensor_copy(srowA[0:1, t0_:t0_ + TH],
                                                     pavA[64:65, :]))
                                if qh == 0 and c >= 1:
                                    v.wait_ge(QW.h, qw0 + 32 * c)
                                DV.inc(v.tensor_copy(srowB[0:1, t0_:t0_ + TH],
                                                     pavB[64:65, :]))

                    @block.tensor
                    def _(t):
                        t.wait_ge(DV.h, dv0 + 1)
                        for j in range(NQH):
                            c, slot = j // 2, j % 2
                            qb = slot * 64
                            kv = 2 * (c // 4) + slot
                            kc2 = kv // 2
                            r = (j % 2) * 32
                            t.wait_ge(QL.h, ql0 + 16 * (c + 1))
                            if j >= 2:
                                t.wait_ge(DV.h, dv0 + j)
                            t.matmul(pshA[r:r + 1, 0:512], kbars[qb:qb + 64, kc2:kc2 + 1],
                                     qchunk[c % 2][qb:qb + 64, 0:512],
                                     start=True, stop=True)
                            t.matmul(pshA[r:r + 1, 512:1024], kbars[qb:qb + 64, kc2:kc2 + 1],
                                     qchunk[c % 2][qb:qb + 64, 512:1024],
                                     start=True, stop=True)
                            t.matmul(pshB[r:r + 1, 0:512], kbars[qb:qb + 64, kc2:kc2 + 1],
                                     qchunk[c % 2][qb:qb + 64, 1024:1536],
                                     start=True, stop=True)
                            last = t.matmul(pshB[r:r + 1, 512:1024],
                                            kbars[qb:qb + 64, kc2:kc2 + 1],
                                            qchunk[c % 2][qb:qb + 64, 1536:2048],
                                            start=True, stop=True)
                            PE.inc(last)
                        t.wait_ge(DV.h, dv0 + 33)
                        for j in range(NQH):
                            ins = t.matmul(pshA[0:128, j:j + 1], onesrow[:, :],
                                           statb[0:1, j:j + 1],
                                           start=True, stop=True)
                            if j >= NQH - 2:
                                PE.inc(ins)
                        t.wait_ge(AC.h, ac0 + 1)
                        ql1 = ql0 + 256
                        for c in range(16):
                            jA, jB = 2 * c, 2 * c + 1
                            kvA = 2 * (c // 4)
                            kvB = kvA + 1
                            kc2 = c // 4
                            pc = pe0 + 34 + 128 * c
                            t.wait_ge(QL.h, ql1 + 16 * (c + 1))
                            for qh in range(2):
                                t0_ = qh * TH
                                bq = pc + 64 * qh
                                ab = ac0 + 1 + 64 * c + 32 * qh

                                def emit_sc(kc):
                                    if kc == 0:
                                        if qh == 0:
                                            t.wait_ge(AC.h, ac0 + 1 + 64 * c)
                                        else:
                                            t.wait_ge(AC.h, ac0 + 1 + 64 * c + 32)
                                    else:
                                        t.wait_ge(AC.h, ab + 2 * kc)
                                    for X, (qb, psx) in enumerate(
                                            ((0, pshA), (64, pshB))):
                                        last = None
                                        for jh in range(2):
                                            last = t.matmul(
                                                psx[:, jh * 512:(jh + 1) * 512],
                                                krview(kc2)[qb:qb + 64,
                                                            kc * 128:(kc + 1) * 128],
                                                qchunk[c % 2][qb:qb + 64,
                                                              t0_ + jh * 512:
                                                              t0_ + (jh + 1) * 512],
                                                start=True, stop=True)
                                        PE.inc(last)

                                def emit_av(kc):
                                    t.wait_ge(AC.h, ab + 2 * kc + 2)
                                    if kc == 0:
                                        if qh == 0:
                                            t.wait_ge(DV.h, dv0 + 33 + 8 * c)
                                        else:
                                            t.wait_ge(DV.h, dv0 + 33 + 8 * c + 4)
                                    for X, (pavx, pbx, kv) in enumerate(
                                            ((pavA, PbA, kvA), (pavB, PbB, kvB))):
                                        last = None
                                        for jh in range(2):
                                            last = t.matmul(
                                                pavx[0:65, jh * 512:(jh + 1) * 512],
                                                vdview(kv, kc)[:, :],
                                                pbx[kc % 2][:, jh * 512:(jh + 1) * 512],
                                                start=(kc == 0), stop=(kc == 15))
                                        PE.inc(last)

                                emit_sc(0)
                                for kc in range(1, 16):
                                    emit_sc(kc)
                                    emit_av(kc - 1)
                                emit_av(15)

                    @block.scalar
                    def _(s):
                        s.wait_ge(PE.h, pe0 + 34)
                        AC.inc(s.activation(negb_all[:, :], pshA[:, 0:32], AF.Copy))
                        for c in range(16):
                            pc = pe0 + 34 + 128 * c
                            for qh in range(2):
                                bq = pc + 64 * qh
                                ab = ac0 + 1 + 64 * c + 32 * qh
                                for kc in range(16):
                                    for X, (psx, pbx, j) in enumerate(
                                            ((pshA, PbA, 2 * c), (pshB, PbB, 2 * c + 1))):
                                        if kc == 0:
                                            s.wait_ge(PE.h, bq + 1 + X)
                                        else:
                                            s.wait_ge(PE.h, bq + 4 * kc - 1 + X)
                                        AC.inc(s.activation(pbx[kc % 2][:, :],
                                                            psx[:, :], AF.Exp,
                                                            bias=negb_all[:, j:j + 1],
                                                            scale=0.125))

                    @block.gpsimd
                    def _(g):
                        for c in range(16):
                            g.wait_ge(DV.h, dv0 + 33 + 8 * c + 7)
                            QW.inc(g.dma_start(out=sall_d[2 * c:2 * c + 1, :],
                                               in_=srowA[0:1, :]), 16)
                            g.wait_ge(DV.h, dv0 + 33 + 8 * c + 8)
                            QW.inc(g.dma_start(out=sall_d[2 * c + 1:2 * c + 2, :],
                                               in_=srowB[0:1, :]), 16)

                    QL.expect(ql0, 512, "heads")
                    QW.expect(qw0, 16 * 32, "heads")
                    PE.expect(pe0, 34 + 128 * 16, "heads")
                    AC.expect(ac0, 1 + 1024, "heads")
                    DV.expect(dv0, 33 + 128, "heads")

                with nc.Block() as block, \
                     nc.psum_tensor("pSb0", [128, SEQ], F32) as pSb0, \
                     nc.psum_tensor("pSb1", [128, SEQ], F32) as pSb1:
                    pSb = [pSb0, pSb1]
                    qw0, ql0, pe0, dv0 = QW.n, QL.n, PE.n, DV.n

                    @block.gpsimd
                    def _(g):
                        QW.inc(g.dma_start(out=eselb[:, :], in_=esel_d[:, :]), 16)

                    @block.sync
                    def _(sync):
                        sync.wait_ge(QW.h, qw0)
                        QL.inc(sync.dma_start(out=SallT[:, :], in_=sall_d[:, :]), 16)

                    @block.vector
                    def _(v):
                        v.wait_ge(QL.h, ql0 + 16)
                        v.reciprocal(SinvT[:, :], SallT[:, :])
                        v.tensor_copy(Shi[:, :], SinvT[:, :])
                        DV.inc(v.scalar_tensor_tensor(Slo[:, :], Shi[:, :], -1.0,
                                                      SinvT[:, :], ALU.mult, ALU.add))
                        for c in range(16):
                            v.wait_ge(PE.h, pe0 + c + 1)
                            DV.inc(v.tensor_mul(attnview(c), attnview(c),
                                                pSb[c % 2][:, :]))

                    @block.tensor
                    def _(t):
                        t.wait_ge(QW.h, qw0 + 16)
                        for c in range(16):
                            t.wait_ge(DV.h, dv0 + 1 if c < 2 else dv0 + c)
                            last = None
                            for j in range(4):
                                sl = slice(j * 512, (j + 1) * 512)
                                t.matmul(pSb[c % 2][:, sl],
                                         eselb[:, c * 128:(c + 1) * 128],
                                         Shi[:, sl], start=True, stop=False)
                                last = t.matmul(pSb[c % 2][:, sl],
                                                eselb[:, c * 128:(c + 1) * 128],
                                                Slo[:, sl], start=False, stop=True)
                            PE.inc(last)

                    PE.expect(pe0, 16, "natt")
                    DV.expect(dv0, 17, "natt")

                with ExitStack() as octx:
                    osb = lambda n, s, d: octx.enter_context(nc.sbuf_tensor(n, s, d))
                    wsb2 = [osb(f"wout{i}", [128, SEQ], BF16) for i in range(2)]
                    r1b = [osb(f"r1b{i}", [128, SEQ], F32) for i in range(2)]
                    r2b = [osb(f"r2b{i}", [128, SEQ], F32) for i in range(2)]

                    with nc.Block() as block, \
                         nc.psum_tensor("po0", [128, SEQ], F32) as po0, \
                         nc.psum_tensor("po1", [128, SEQ], F32) as po1:
                        po = [po0, po1]
                        qw0, ql0, qs0, pe0, dv0 = QW.n, QL.n, QS.n, PE.n, DV.n

                        @block.gpsimd
                        def _(g):
                            for oc in range(NHC):
                                if oc >= 2:
                                    g.wait_ge(PE.h, pe0 + oc - 1)
                                QW.inc(g.dma_start(
                                    out=wsb2[oc % 2][:, :],
                                    in_=woutt_d[oc * 128:(oc + 1) * 128, :]), 16)

                        @block.sync
                        def _(sync):
                            for oc in range(NHC):
                                if oc >= 2:
                                    sync.wait_ge(DV.h, dv0 + oc - 1)
                                QL.inc(sync.dma_start(
                                    out=r1b[oc % 2][:, :],
                                    in_=r1_d[oc * 128:(oc + 1) * 128, :]), 16)

                        @block.tensor
                        def _(t):
                            for oc in range(NHC):
                                t.wait_ge(QW.h, qw0 + 16 * (oc + 1))
                                if oc >= 2:
                                    t.wait_ge(DV.h, dv0 + oc - 1)
                                last = None
                                for fc in range(NHC):
                                    for j in range(4):
                                        last = t.matmul(
                                            po[oc % 2][:, j * 512:(j + 1) * 512],
                                            wsb2[oc % 2][:, fc * 128:(fc + 1) * 128],
                                            attnview(fc)[:, j * 512:(j + 1) * 512],
                                            start=(fc == 0), stop=(fc == NHC - 1))
                                PE.inc(last)

                        @block.vector
                        def _(v):
                            for oc in range(NHC):
                                v.wait_ge(PE.h, pe0 + oc + 1)
                                v.wait_ge(QL.h, ql0 + 16 * (oc + 1))
                                if oc >= 2:
                                    v.wait_ge(QS.h, qs0 + 16 * (oc - 1))
                                DV.inc(v.scalar_tensor_tensor(
                                    r2b[oc % 2][:, :], po[oc % 2][:, :],
                                    sm(24 + oc), r1b[oc % 2][:, :],
                                    ALU.add, ALU.add))

                        @block.scalar
                        def _(s):
                            for oc in range(NHC):
                                s.wait_ge(DV.h, dv0 + oc + 1)
                                QS.inc(s.dma_start(
                                    out=r2_d[oc * 128:(oc + 1) * 128, :],
                                    in_=r2b[oc % 2][:, :]), 16)

                        PE.expect(pe0, NHC, "oproj")
                        DV.expect(dv0, NHC, "oproj")
                        QS.expect(qs0, 16 * NHC, "oproj")

        qs_after_attn = QS.n

        if dbg == "attn":
            with nc.Block() as block:
                @block.sync
                def _(sync):
                    sync.wait_ge(QS.h, QS.n)
                    sync.wait_ge(QW.h, QW.n)
                    sync.wait_ge(QL.h, QL.n)
            return nc

        ffn_phase(r2_d, out_d, "f2", wait_qs=qs_after_attn)

        with nc.Block() as block:
            @block.sync
            def _(sync):
                sync.wait_ge(QS.h, QS.n)
                sync.wait_ge(QW.h, QW.n)
                sync.wait_ge(QL.h, QL.n)

    return nc




def kernel(**inputs):
    x = np.asarray(inputs["L0_attn_x"], dtype=np.float32)
    consts = host_layout(inputs)
    nc = build()
    in_maps = []
    for b in range(8):
        m = dict(consts)
        m["xT"] = np.ascontiguousarray(x[b].T)
        in_maps.append(m)
    res = run_bass_kernel_spmd(nc, in_maps, core_ids=list(range(8)))
    out = np.stack([np.asarray(res.results[b]["outT"]).T for b in range(8)], axis=0)
    return np.ascontiguousarray(out.astype(np.float32))
